# version 02 (final)
# baseline (speedup 1.0000x reference)
import math
import numpy as np
import ml_dtypes
import concourse.bass as bass
import concourse.mybir as mybir
from concourse.bass import IndirectOffsetOnAxis
from concourse.bass_utils import run_bass_kernel_spmd

F32 = mybir.dt.float32
BF16 = mybir.dt.bfloat16
U32 = mybir.dt.uint32
AF = mybir.ActivationFunctionType
ALU = mybir.AluOpType
AX = mybir.AxisListType

NCORES = 8
NBLK = 8
BLK = 256
NT = 16
TOK = NBLK * BLK
D = 2048
DH = 128
HQ = 16
HKV = 4
WIN = 512
L = 256
E = 32
H = 1024
CAP = 1024
SCALE = 2.5
LN_EPS = 1e-5
RMS_EPS = 1.1920929e-07
MB = 11.35
ISQ = 1.0 / math.sqrt(128.0)
BIGA = 1.0e6
BIG = 1.0e9

BF = ml_dtypes.bfloat16


class T:
    __slots__ = ("name", "w", "readers", "relax_waw")

    def __init__(self, name, relax_waw=False):
        self.name = name
        self.w = {}
        self.readers = []
        self.relax_waw = relax_waw


class B:
    ENGINES = ("tensor", "vector", "scalar", "sync", "gpsimd")

    def __init__(self):
        self.events = []
        self.cnt = {e: 0 for e in self.ENGINES}
        self.cnt["cc"] = 0

    def op(self, eng, method, args=(), kwargs=None, reads=(), writes=(),
           deps=(), dma=False, inc_sem=None, inc=None):
        kwargs = kwargs or {}
        dl = list(deps)
        for t in reads:
            dl.extend(t.w.items())
        for t in writes:
            for r in t.readers:
                dl.append(r)
            if not t.relax_waw:
                dl.extend(t.w.items())
        if inc_sem is None and dma:
            inc_sem = ("D" + (writes[0].name if writes else "misc")
                       + ("@g" if eng == "gpsimd" else ""))
        sem_key = inc_sem or eng
        if sem_key not in self.cnt:
            self.cnt[sem_key] = 0
        amt = inc if inc is not None else (16 if dma else 1)
        self.cnt[sem_key] += amt
        tok = (sem_key, self.cnt[sem_key])
        self.events.append((eng, dma, method, args, kwargs, dl, amt, sem_key))
        for t in reads:
            t.readers.append(tok)
        for t in writes:
            t.w[tok[0]] = tok[1]
            t.readers = []
        return tok

    def barrier(self):
        snap = [(s, v) for s, v in self.cnt.items() if v > 0]
        for e in self.ENGINES:
            self.events.append((e, False, None, (), {}, list(snap), 0, e))

    def plan_for(self, ename):
        waited = {}
        out = []
        for (e, dma, method, args, kwargs, deps, amt, sem_key) in self.events:
            if e != ename:
                continue
            waits = []
            for (s, v) in deps:
                if SKIP_SAME and s == ename and ename in (
                        "tensor", "vector", "scalar") and not dma:
                    continue
                if waited.get(s, 0) >= v:
                    continue
                waited[s] = v
                waits.append((s, v))
            out.append((waits, dma, method, args, kwargs, amt, sem_key))
        return out


def _np_rope_tables():
    inv = (1.0 / 10000.0 ** (np.arange(0, DH, 2, dtype=np.float32) / DH)
           ).astype(np.float32)
    t = np.arange(4096, dtype=np.float32)
    fr = t[:, None] * inv[None, :]
    cos = np.repeat(np.cos(fr), 2, axis=1).astype(np.float32)
    sin = np.sin(fr).astype(np.float32)
    sins = np.empty((4096, DH), np.float32)
    sins[:, 0::2] = -sin
    sins[:, 1::2] = sin
    return cos, sins


def _host_prep(inputs):
    x = np.asarray(inputs["L0_attn_x"])
    xp = np.ascontiguousarray(x.reshape(16384, D))
    cos, sins = _np_rope_tables()

    wqkvT = np.ascontiguousarray(np.asarray(inputs["L0_attn_w_qkv"]).T)
    wprojT = np.ascontiguousarray(np.asarray(inputs["L0_attn_w_proj"]).T)
    gateT = np.ascontiguousarray(np.asarray(inputs["L0_ffn_gate_w"]).T)
    shupT = np.ascontiguousarray(np.asarray(inputs["L0_ffn_shared_up_w"]).T)
    shdnT = np.ascontiguousarray(np.asarray(inputs["L0_ffn_shared_down_w"]).T)
    upT = np.ascontiguousarray(np.asarray(inputs["L0_ffn_up_w"]).transpose(0, 2, 1))
    dnT = np.ascontiguousarray(np.asarray(inputs["L0_ffn_down_w"]).transpose(0, 2, 1))
    bias2 = np.ascontiguousarray(np.asarray(inputs["L0_ffn_bias_corr"]).reshape(1, E))

    eye_bf = np.eye(128, dtype=np.float32).astype(BF)
    tp, tq = np.meshgrid(np.arange(128), np.arange(128), indexing="ij")
    lt = (tp < tq).astype(np.float32).astype(BF)
    ones128 = np.ones((128, 128), np.float32).astype(BF)
    onescol = np.ones((128, 1), np.float32).astype(BF)
    onesrow32 = np.ones((1, 128), np.float32)
    iota32 = np.arange(E, dtype=np.float32).reshape(1, E)
    iota128 = np.arange(128, dtype=np.float32).reshape(128, 1)

    in_maps = []
    core_blocks = []
    for c in range(NCORES):
        gs = [c + 8 * j for j in range(NBLK)]
        core_blocks.append(gs)
        rows = np.concatenate([np.arange(g * BLK, (g + 1) * BLK) for g in gs])
        x_own = np.ascontiguousarray(xp[rows])

        x_halo = np.zeros((NBLK, WIN, D), np.float32)
        masksT = np.zeros((NBLK, 768, BLK), np.float32)
        cos_own = np.zeros((TOK, DH), np.float32)
        sin_own = np.zeros((TOK, DH), np.float32)
        cos_h = np.zeros((NBLK * WIN, DH), np.float32)
        sin_h = np.zeros((NBLK * WIN, DH), np.float32)
        for j, g in enumerate(gs):
            lblk = g % 16
            h0 = lblk * BLK - WIN
            lo = max(0, -h0)
            if lo < WIN:
                x_halo[j, lo:] = xp[g * BLK - WIN + lo: g * BLK]
            cos_h[j * WIN: (j + 1) * WIN] = cos[np.maximum(h0 + np.arange(WIN), 0)]
            sin_h[j * WIN: (j + 1) * WIN] = sins[np.maximum(h0 + np.arange(WIN), 0)]
            cos_own[j * BLK:(j + 1) * BLK] = cos[lblk * BLK:(lblk + 1) * BLK]
            sin_own[j * BLK:(j + 1) * BLK] = sins[lblk * BLK:(lblk + 1) * BLK]
            kj = np.arange(768)[:, None]
            qi = np.arange(BLK)[None, :]
            p_k = h0 + kj
            dd = (qi + WIN) - kj
            masksT[j] = ((p_k >= 0) & (dd >= 0) & (dd < WIN)).astype(np.float32)
        mmat = np.zeros((64, NBLK), np.float32)
        place = np.zeros((NBLK, 64), np.float32)
        for j, g in enumerate(gs):
            mmat[:g, j] = 1.0
            place[j, g] = 1.0

        in_maps.append(dict(
            x_own=x_own,
            x_halo=np.ascontiguousarray(x_halo.reshape(NBLK * WIN, D)),
            masksT=np.ascontiguousarray(masksT.astype(BF).reshape(NBLK * 768, BLK)),
            cos_own=cos_own, sin_own=sin_own, cos_h=cos_h, sin_h=sin_h,
            mmat=mmat.astype(BF), placeT=place.astype(BF),
            wqkvT=wqkvT, wprojT=wprojT, gateT=gateT, shupT=shupT, shdnT=shdnT,
            upT=upT, dnT=dnT, bias2=bias2,
            eye_bf=eye_bf, eye32=np.eye(128, dtype=np.float32),
            ltm=np.asarray(lt), ones128=ones128, onescol=onescol,
            onesrow32=onesrow32, iota32=iota32, iota128=iota128,
        ))
    return in_maps, core_blocks


SUB = None
DEBUG = False
SKIP_SAME = False


def build_nc(nphase=7):
    nc = bass.Bass("TRN2", target_bir_lowering=False)
    b = B()

    def din(name, shape, dt):
        return nc.dram_tensor(name, shape, dt, kind="ExternalInput").ap()

    x_own = din("x_own", [TOK, D], F32)
    x_halo = din("x_halo", [NBLK * WIN, D], F32)
    masksT = din("masksT", [NBLK * 768, BLK], BF16)
    cos_own = din("cos_own", [TOK, DH], F32)
    sin_own = din("sin_own", [TOK, DH], F32)
    cos_h = din("cos_h", [NBLK * WIN, DH], F32)
    sin_h = din("sin_h", [NBLK * WIN, DH], F32)
    mmat = din("mmat", [64, NBLK], BF16)
    placeT = din("placeT", [NBLK, 64], BF16)
    wqkvT = din("wqkvT", [D, 3072], F32)
    wprojT = din("wprojT", [D, D], F32)
    gateT = din("gateT", [D, E], F32)
    shupT = din("shupT", [D, 4096], F32)
    shdnT = din("shdnT", [4096, D], F32)
    upT = din("upT", [E, D, H], F32)
    dnT = din("dnT", [E, H, D], F32)
    bias2 = din("bias2", [1, E], F32)
    eye_bf_in = din("eye_bf", [128, 128], BF16)
    eye32_in = din("eye32", [128, 128], F32)
    ltm_in = din("ltm", [128, 128], BF16)
    ones128_in = din("ones128", [128, 128], BF16)
    onescol_in = din("onescol", [128, 1], BF16)
    onesrow32_in = din("onesrow32", [1, 128], F32)
    iota32_in = din("iota32", [1, E], F32)
    iota128_in = din("iota128", [128, 1], F32)

    out_d = nc.dram_tensor("out", [TOK, D], F32, kind="ExternalOutput").ap()

    def dsc(name, shape, dt, **kw):
        return nc.dram_tensor(name, shape, dt, **kw).ap()
    xn_tm = dsc("xn_tm", [TOK, D], BF16)
    xnfm_d = dsc("xnfm_d", [NT, 128, 16, 128], BF16)
    qst = dsc("qst", [TOK, D], BF16)
    kst = dsc("kst", [TOK, 512], BF16)
    vst = dsc("vst", [TOK, 512], BF16)
    khst = dsc("khst", [NBLK * WIN, 512], BF16)
    vhst = dsc("vhst", [NBLK * WIN, 512], BF16)
    s_st = dsc("s_st", [TOK, D], BF16)
    shacc = dsc("shacc", [TOK, D], F32)
    lists_d = dsc("lists_d", [E * L, 2], F32)
    ye_all = dsc("ye_all", [E * L, D], BF16)
    cc_in = dsc("cc_in", [64, E], F32)
    cc_out = dsc("cc_out", [64, E], F32)
    offs_d = dsc("offs_d", [NBLK, E], F32)

    d_xn = T("d_xn", True); d_xnfm = T("d_xnfm", True); d_q = T("d_q", True)
    d_k = T("d_k", True); d_v = T("d_v", True); d_kh = T("d_kh", True)
    d_vh = T("d_vh", True); d_s = T("d_s", True); d_sh = T("d_sh")
    d_li = T("d_li", True); d_ye = T("d_ye", True); d_ci = T("d_ci")
    d_co = T("d_co"); d_of = T("d_of"); d_out = T("d_out", True)

    from contextlib import ExitStack
    st = ExitStack()
    Ecm = st.enter_context

    def sb(name, shape, dt):
        ap = Ecm(nc.sbuf_tensor(name, shape, dt))
        return ap, T(name)

    def ps(name, shape, dt):
        ap = Ecm(nc.psum_tensor(name, shape, dt))
        return ap, T(name)

    c_eyebf, t_eyebf = sb("c_eyebf", [128, 128], BF16)
    c_eye32, t_eye32 = sb("c_eye32", [128, 128], F32)
    c_lt, t_lt = sb("c_lt", [128, 128], BF16)
    c_o128, t_o128 = sb("c_o128", [128, 128], BF16)
    c_ocol, t_ocol = sb("c_ocol", [128, 1], BF16)
    c_orow32, t_orow32 = sb("c_orow32", [1, 128], F32)
    c_iota32, t_iota32 = sb("c_iota32", [128, E], F32)
    c_bias, t_bias = sb("c_bias", [128, E], F32)
    c_iota128, t_iota128 = sb("c_iota128", [128, 1], F32)
    c_neg1, t_neg1 = sb("c_neg1", [128, 1], F32)
    c_nhalf, t_nhalf = sb("c_nhalf", [128, 1], F32)
    c_half, t_half = sb("c_half", [128, 1], F32)
    c_mb, t_mb = sb("c_mb", [128, 1], F32)
    gate_sb, t_gate = sb("gate_sb", [128, 16 * E], F32)
    selbf, t_selbf = sb("selbf", [128, NT * E], BF16)
    self32, t_self32 = sb("self32", [128, NT * E], F32)
    vldbf, t_vldbf = sb("vldbf", [128, NT * E], BF16)
    vld32, t_vld32 = sb("vld32", [128, NT * E], F32)
    estore, t_est = sb("estore", [128, NT * 4], F32)
    wstore, t_wst = sb("wstore", [128, NT * 4], F32)
    addrs, t_addr = sb("addrs", [128, NT * 4], U32)
    cnt64, t_c64 = sb("cnt64", [64, E], F32)
    cnt64b, t_c64b = sb("cnt64b", [64, E], BF16)
    cnt64f, t_c64f = sb("cnt64f", [64, E], F32)
    mm_sb, t_mm = sb("mm_sb", [64, NBLK], BF16)
    place_sb, t_place = sb("place_sb", [NBLK, 64], BF16)
    offs_sb, t_offs = sb("offs_sb", [NBLK, E], F32)
    obc, t_obc = sb("obc", [128, E], F32)
    r32a, t_r32a = sb("r32a", [128, E], F32)
    r32b, t_r32b = sb("r32b", [128, E], F32)
    r32c, t_r32c = sb("r32c", [128, E], F32)
    rg48, t_rg48 = sb("rg48", [128, 48], F32)
    rg8a, t_rg8a = sb("rg8a", [128, 8], F32)
    rg8b, t_rg8b = sb("rg8b", [128, 8], F32)
    rcol, t_rcol = sb("rcol", [128, 12], F32)
    sdats = [sb("sdat%d" % q, [128, 2], F32) for q in range(4)]
    tloc, t_tloc = sb("tloc", [128, 1], F32)
    tabc, t_tabc = sb("tabc", [128, DH], F32)
    tabs_, t_tabs = sb("tabs", [128, DH], F32)
    ms16, t_ms16 = sb("ms16", [128, 16], F32)
    ln8, t_ln8 = sb("ln8", [128, 8], F32)
    denrow, t_den = sb("denrow", [1, BLK], F32)
    recrow, t_rec = sb("recrow", [1, BLK], F32)
    lists_t, t_lit = sb("lists_t", [128, 4], F32)
    idx_u, t_idx = sb("idx_u", [128, 2], U32)

    pf32, _ = sb("pf32", [128, 11264], F32)
    pbf, _ = sb("pbf", [128, 26624], BF16)
    wbuf, _ = sb("wbuf", [128, 49152], BF16)

    def carve(pool, off, n, name):
        return pool[:, off:off + n], T(name)

    x_t, t_x = carve(pf32, 0, 2048, "x_t")
    f32a, t_fa = carve(pf32, 2048, 2048, "f32a")
    f32b, t_fb = carve(pf32, 4096, 2048, "f32b")
    xn_t, t_xn = carve(pf32, 6144, 2048, "xn_t")
    xnf32, t_xnf32 = carve(pf32, 8192, 2048, "xnf32")
    k32, t_k32 = carve(pf32, 10240, 512, "k32")
    cnt_row, t_cnt_row = carve(pf32, 10752, 256, "cnt_row")
    relu32, t_rel = carve(pf32, 10240, 512, "relu32")
    shal, t_shal = carve(pf32, 8192, 2048, "shal")
    sld, t_sld = carve(pf32, 6144, 2048, "sld")

    xnfbf, t_xnfbf = carve(pbf, 0, 2048, "xnfbf")
    qbf, t_qbf = carve(pbf, 2048, 2048, "qbf")
    kbf, t_kbf = carve(pbf, 4096, 512, "kbf")
    vbf_s, t_vbf = carve(pbf, 4608, 512, "vbf")

    qin, t_qin = carve(pbf, 0, 2048, "qin")
    kin, t_kin = carve(pbf, 2048, 3072, "kin")
    t_kin.relax_waw = True
    vin, t_vin = carve(pbf, 5120, 3072, "vin")
    t_vin.relax_waw = True
    qfm, t_qfm = carve(pbf, 8192, 4096, "qfm")
    kfm, t_kfm = carve(pbf, 12288, 3072, "kfm")
    probs, t_pr = carve(pbf, 15360, 1536, "probs")
    masksb, t_msk = carve(pbf, 16896, 1536, "masksb")
    yraw, t_yraw = carve(pbf, 18432, 4096, "yraw")
    sbf, t_sbf = carve(pbf, 22528, 2048, "sbf")
    recbc, t_rbc = carve(pbf, 24576, 256, "recbc")

    xng, t_xng = carve(pbf, 0, 8192, "xng")
    hfm, t_hfm = carve(pbf, 8192, 4096, "hfm")

    xe_tm, t_xet = carve(pbf, 0, 2048, "xe_tm")
    xe_fm, t_xef = carve(pbf, 2048, 4096, "xe_fm")
    he_bf, t_he = carve(pbf, 6144, 2048, "he_bf")
    ye_sb, t_ye = carve(pbf, 8192, 2048, "ye_sb")

    t_ws = [T("wslot%d" % q) for q in range(3)]

    def wbq(lo, hi):
        return tuple(t_ws[q] for q in range(3)
                     if lo < (q + 1) * 16384 and hi > q * 16384)

    pm, t_pm = ps("pm", [128, 2048], F32)
    psc, t_psc = ps("psc", [128, 512], F32)
    py, t_py = ps("py", [128, 512], F32)
    pbf_ps, t_pbf = ps("pbf_ps", [128, 512], F32)
    prow, t_prow = ps("prow", [1, 512], F32)
    poff, t_poff = py[0:NBLK, 0:E], t_py
    pl64, t_pl64 = py[0:64, 0:E], t_py

    def dma(queue, out_ap, in_ap, reads=(), writes=(), deps=(), **kw):
        return b.op(queue, "dma_start", kwargs=dict(out=out_ap, in_=in_ap, **kw),
                    reads=reads, writes=writes, deps=deps, dma=True)

    def finish():
        if DEBUG:
            env = dict(xn_tm=xn_tm, qst=qst, kst=kst, vst=vst, khst=khst,
                       vhst=vhst, s_st=s_st, shacc=shacc, lists_d=lists_d,
                       ye_all=ye_all, cc_out=cc_out, offs_d=offs_d,
                       d_xn=d_xn, d_q=d_q, d_k=d_k, d_v=d_v, d_kh=d_kh,
                       d_vh=d_vh, d_s=d_s, d_sh=d_sh, d_li=d_li, d_ye=d_ye,
                       d_co=d_co, d_of=d_of,
                       addrs=addrs, estore=estore, wstore=wstore,
                       self32=self32, vld32=vld32,
                       t_addr=t_addr, t_est=t_est, t_wst=t_wst,
                       t_self32=t_self32, t_vld32=t_vld32)
            _emit_debug(nc, b, dma, env)
        return _build_finish(nc, b, st)

    dma("sync", c_eyebf[:], eye_bf_in, writes=(t_eyebf,))
    dma("sync", c_eye32[:], eye32_in, writes=(t_eye32,))
    dma("sync", c_lt[:], ltm_in, writes=(t_lt,))
    dma("sync", c_o128[:], ones128_in, writes=(t_o128,))
    dma("sync", c_ocol[:], onescol_in, writes=(t_ocol,))
    dma("sync", c_orow32[:], onesrow32_in, writes=(t_orow32,))
    dma("sync", c_iota32[:], iota32_in.broadcast_to([128, E]), writes=(t_iota32,))
    dma("sync", c_bias[:], bias2.broadcast_to([128, E]), writes=(t_bias,))
    dma("sync", c_iota128[:], iota128_in, writes=(t_iota128,))
    dma("sync", mm_sb[:], mmat, writes=(t_mm,))
    dma("sync", place_sb[:], placeT, writes=(t_place,))
    b.op("vector", "memset", (c_neg1[:], -1.0), writes=(t_neg1,))
    b.op("vector", "memset", (c_nhalf[:], -0.5), writes=(t_nhalf,))
    b.op("vector", "memset", (c_half[:], 0.5), writes=(t_half,))
    b.op("vector", "memset", (c_mb[:], -MB), writes=(t_mb,))
    b.op("vector", "memset", (f32a[:, 0:128], BIGA), writes=(t_fa,))
    dma("sync", lists_d, f32a[:, 0:128], reads=(t_fa,), writes=(d_li,))
    dma("sync", gate_sb[:].rearrange("p (c e) -> p c e", c=16),
        gateT.rearrange("(c p) e -> p c e", p=128), writes=(t_gate,))
    wq = wbuf[:, 0:49152].rearrange("p (c f) -> p c f", c=16)
    dma("gpsimd", wq[:, :, 0:1536],
        wqkvT.rearrange("(c p) f -> p c f", p=128)[:, :, 0:1536],
        writes=wbq(0, 49152))
    dma("gpsimd", wq[:, :, 1536:3072],
        wqkvT.rearrange("(c p) f -> p c f", p=128)[:, :, 1536:3072],
        writes=wbq(0, 49152))

    if SUB == "p0":
        return finish()

    def ln_tile():
        b.op("vector", "reduce_sum", (ln8[:, 0:1], x_t[:]),
             kwargs=dict(axis=AX.X), reads=(t_x,), writes=(t_ln8,))
        b.op("vector", "tensor_mul", (f32a[:], x_t[:], x_t[:]),
             reads=(t_x,), writes=(t_fa,))
        b.op("vector", "reduce_sum", (ln8[:, 1:2], f32a[:]),
             kwargs=dict(axis=AX.X), reads=(t_fa,), writes=(t_ln8,))
        b.op("vector", "tensor_scalar_mul", (ln8[:, 2:3], ln8[:, 0:1], 1.0 / D),
             reads=(t_ln8,), writes=(t_ln8,))
        b.op("vector", "tensor_scalar_mul", (ln8[:, 3:4], ln8[:, 1:2], 1.0 / D),
             reads=(t_ln8,), writes=(t_ln8,))
        b.op("vector", "tensor_mul", (ln8[:, 4:5], ln8[:, 2:3], ln8[:, 2:3]),
             reads=(t_ln8,), writes=(t_ln8,))
        b.op("vector", "tensor_sub", (ln8[:, 5:6], ln8[:, 3:4], ln8[:, 4:5]),
             reads=(t_ln8,), writes=(t_ln8,))
        b.op("vector", "tensor_scalar_add", (ln8[:, 5:6], ln8[:, 5:6], LN_EPS),
             reads=(t_ln8,), writes=(t_ln8,))
        b.op("scalar", "activation", (ln8[:, 6:7], ln8[:, 5:6], AF.Ln),
             reads=(t_ln8,), writes=(t_ln8,))
        b.op("scalar", "activation", (ln8[:, 7:8], ln8[:, 6:7], AF.Exp),
             kwargs=dict(scale=c_nhalf[:]), reads=(t_ln8, t_nhalf), writes=(t_ln8,))
        b.op("vector", "tensor_scalar",
             (xn_t[:], x_t[:], ln8[:, 2:3], ln8[:, 7:8], ALU.subtract, ALU.mult),
             reads=(t_x, t_ln8), writes=(t_xn,))

    def transpose_xn(to_f32):
        for c4 in range(4):
            for u in range(4):
                c = c4 * 4 + u
                b.op("tensor", "matmul",
                     (psc[:, u * 128:(u + 1) * 128], xn_t[:, c * 128:(c + 1) * 128],
                      c_eye32[:]),
                     kwargs=dict(start=True, stop=True),
                     reads=(t_xn, t_eye32), writes=(t_psc,))
            for u in range(4):
                c = c4 * 4 + u
                b.op("vector", "tensor_copy",
                     (xnfbf[:, c * 128:(c + 1) * 128], psc[:, u * 128:(u + 1) * 128]),
                     reads=(t_psc,), writes=(t_xnfbf,))
                if to_f32:
                    b.op("vector", "tensor_copy",
                         (xnf32[:, c * 128:(c + 1) * 128], psc[:, u * 128:(u + 1) * 128]),
                         reads=(t_psc,), writes=(t_xnf32,))

    def rope_rms(src_ap, w, nh, fold_isq, out_ap, t_out, t_src):
        sv = src_ap.rearrange("p (h d) -> p h d", h=nh)
        av = f32a[:, 0:w].rearrange("p (h d) -> p h d", h=nh)
        bv = f32b[:, 0:w].rearrange("p (h d) -> p h d", h=nh)
        cosb = tabc[:].unsqueeze(1).broadcast_to([128, nh, DH])
        sinb = tabs_[:].unsqueeze(1).broadcast_to([128, nh, DH])
        b.op("vector", "tensor_tensor", (av, sv, cosb, ALU.mult),
             reads=(t_src, t_tabc), writes=(t_fa,))
        b.op("vector", "tensor_tensor",
             (bv[:, :, 0:DH:2], sv[:, :, 1:DH:2], sinb[:, :, 0:DH:2], ALU.mult),
             reads=(t_src, t_tabs), writes=(t_fb,))
        b.op("vector", "tensor_tensor",
             (bv[:, :, 1:DH:2], sv[:, :, 0:DH:2], sinb[:, :, 1:DH:2], ALU.mult),
             reads=(t_src, t_tabs), writes=(t_fb,))
        b.op("vector", "tensor_add",
             (f32a[:, 0:w], f32a[:, 0:w], f32b[:, 0:w]),
             reads=(t_fa, t_fb), writes=(t_fa,))
        b.op("vector", "tensor_mul", (f32b[:, 0:w], f32a[:, 0:w], f32a[:, 0:w]),
             reads=(t_fa,), writes=(t_fb,))
        b.op("vector", "reduce_sum",
             (ms16[:, 0:nh], f32b[:, 0:w].rearrange("p (h d) -> p h d", h=nh)),
             kwargs=dict(axis=AX.X), reads=(t_fb,), writes=(t_ms16,))
        b.op("vector", "tensor_scalar",
             (ms16[:, 0:nh], ms16[:, 0:nh], 1.0 / DH, RMS_EPS, ALU.mult, ALU.add),
             reads=(t_ms16,), writes=(t_ms16,))
        b.op("scalar", "activation", (ms16[:, 0:nh], ms16[:, 0:nh], AF.Ln),
             reads=(t_ms16,), writes=(t_ms16,))
        b.op("scalar", "activation", (ms16[:, 0:nh], ms16[:, 0:nh], AF.Exp),
             kwargs=dict(scale=c_nhalf[:]), reads=(t_ms16, t_nhalf), writes=(t_ms16,))
        if fold_isq:
            b.op("vector", "tensor_scalar_mul", (ms16[:, 0:nh], ms16[:, 0:nh], ISQ),
                 reads=(t_ms16,), writes=(t_ms16,))
        msb = ms16[:, 0:nh].unsqueeze(2).broadcast_to([128, nh, DH])
        ov = out_ap.rearrange("p (h d) -> p h d", h=nh)
        b.op("vector", "tensor_tensor", (ov, av, msb, ALU.mult),
             reads=(t_fa, t_ms16), writes=(t_out,))

    def kv_pass():
        for c in range(16):
            b.op("tensor", "ldweights", (xnfbf[:, c * 128:(c + 1) * 128],),
                 reads=(t_xnfbf,))
            for f in (4, 5):
                b.op("tensor", "matmul",
                     (pm[:, (f - 4) * 512:(f - 3) * 512],
                      xnfbf[:, c * 128:(c + 1) * 128],
                      wbuf[:, c * 3072 + f * 512: c * 3072 + (f + 1) * 512]),
                     kwargs=dict(start=(c == 0), stop=(c == 15)),
                     reads=(t_xnfbf,) + wbq(0, 49152), writes=(t_pm,))
        b.op("vector", "tensor_copy", (k32[:], pm[:, 0:512]),
             reads=(t_pm,), writes=(t_k32,))
        b.op("vector", "tensor_copy", (vbf_s[:], pm[:, 512:1024]),
             reads=(t_pm,), writes=(t_vbf,))

    def routing_r1(i):
        sc = r32a; cwb = r32b; msk = r32c
        b.op("scalar", "activation", (sc[:], py[:, 0:E], AF.Exp),
             kwargs=dict(scale=c_neg1[:]), reads=(t_py, t_neg1), writes=(t_r32a,))
        b.op("vector", "tensor_scalar_add", (sc[:], sc[:], 1.0),
             reads=(t_r32a,), writes=(t_r32a,))
        b.op("vector", "reciprocal", (sc[:], sc[:]), reads=(t_r32a,), writes=(t_r32a,))
        b.op("vector", "tensor_add", (cwb[:], sc[:], c_bias[:]),
             reads=(t_r32a, t_bias), writes=(t_r32b,))
        pairs = [(0, 1), (0, 2), (0, 3), (1, 2), (1, 3), (2, 3)]
        for k6, (aa, bb) in enumerate(pairs):
            b.op("vector", "tensor_add",
                 (rg48[:, k6 * 8:(k6 + 1) * 8], cwb[:, aa::4], cwb[:, bb::4]),
                 reads=(t_r32b,), writes=(t_rg48,))
        b.op("vector", "tensor_max", (rg48[:, 0:8], rg48[:, 0:8], rg48[:, 8:16]),
             reads=(t_rg48,), writes=(t_rg48,))
        b.op("vector", "tensor_max", (rg48[:, 16:24], rg48[:, 16:24], rg48[:, 24:32]),
             reads=(t_rg48,), writes=(t_rg48,))
        b.op("vector", "tensor_max", (rg48[:, 32:40], rg48[:, 32:40], rg48[:, 40:48]),
             reads=(t_rg48,), writes=(t_rg48,))
        b.op("vector", "tensor_max", (rg48[:, 0:8], rg48[:, 0:8], rg48[:, 16:24]),
             reads=(t_rg48,), writes=(t_rg48,))
        b.op("vector", "tensor_max", (rg8a[:], rg48[:, 0:8], rg48[:, 32:40]),
             reads=(t_rg48,), writes=(t_rg8a,))
        b.op("vector", "tensor_copy", (rg8b[:], rg8a[:]), reads=(t_rg8a,), writes=(t_rg8b,))
        for r in range(4):
            b.op("vector", "reduce_max", (rcol[:, 0:1], rg8b[:]),
                 kwargs=dict(axis=AX.X), reads=(t_rg8b,), writes=(t_rcol,))
            if r < 3:
                b.op("vector", "tensor_scalar",
                     (rg48[:, 0:8], rg8b[:], rcol[:, 0:1], None, ALU.is_ge),
                     reads=(t_rg8b, t_rcol), writes=(t_rg48,))
                b.op("vector", "tensor_scalar_mul", (rg48[:, 0:8], rg48[:, 0:8], BIG),
                     reads=(t_rg48,), writes=(t_rg48,))
                b.op("vector", "tensor_sub", (rg8b[:], rg8b[:], rg48[:, 0:8]),
                     reads=(t_rg8b, t_rg48), writes=(t_rg8b,))
        b.op("vector", "tensor_scalar",
             (rg8a[:], rg8a[:], rcol[:, 0:1], None, ALU.is_ge),
             reads=(t_rg8a, t_rcol), writes=(t_rg8a,))
        gmb = rg8a[:].unsqueeze(2).broadcast_to([128, 8, 4])
        b.op("vector", "tensor_tensor",
             (msk[:].rearrange("p (g i) -> p g i", g=8),
              cwb[:].rearrange("p (g i) -> p g i", g=8), gmb, ALU.mult),
             reads=(t_r32b, t_rg8a), writes=(t_r32c,))
        for k in range(4):
            b.op("vector", "reduce_max", (rcol[:, 1:2], msk[:]),
                 kwargs=dict(axis=AX.X), reads=(t_r32c,), writes=(t_rcol,))
            eq = rg48[:, 0:32]
            b.op("vector", "tensor_scalar",
                 (eq, msk[:], rcol[:, 1:2], None, ALU.is_ge),
                 reads=(t_r32c, t_rcol), writes=(t_rg48,))
            b.op("vector", "tensor_mul", (cwb[:], eq, c_iota32[:]),
                 reads=(t_rg48, t_iota32), writes=(t_r32b,))
            b.op("vector", "reduce_sum", (estore[:, i * 4 + k:i * 4 + k + 1], cwb[:]),
                 kwargs=dict(axis=AX.X), reads=(t_r32b,), writes=(t_est,))
            b.op("vector", "tensor_mul", (cwb[:], eq, sc[:]),
                 reads=(t_rg48, t_r32a), writes=(t_r32b,))
            b.op("vector", "reduce_sum", (wstore[:, i * 4 + k:i * 4 + k + 1], cwb[:]),
                 kwargs=dict(axis=AX.X), reads=(t_r32b,), writes=(t_wst,))
            if k == 0:
                b.op("vector", "tensor_copy", (self32[:, i * E:(i + 1) * E], eq),
                     reads=(t_rg48,), writes=(t_self32,))
            else:
                b.op("vector", "tensor_add",
                     (self32[:, i * E:(i + 1) * E], self32[:, i * E:(i + 1) * E], eq),
                     reads=(t_rg48, t_self32), writes=(t_self32,))
            if k < 3:
                b.op("vector", "tensor_scalar_mul", (cwb[:], eq, BIG),
                     reads=(t_rg48,), writes=(t_r32b,))
                b.op("vector", "tensor_sub", (msk[:], msk[:], cwb[:]),
                     reads=(t_r32c, t_r32b), writes=(t_r32c,))
        w0 = wstore[:, i * 4:i * 4 + 1]; w1 = wstore[:, i * 4 + 1:i * 4 + 2]
        w2 = wstore[:, i * 4 + 2:i * 4 + 3]; w3 = wstore[:, i * 4 + 3:i * 4 + 4]
        b.op("vector", "tensor_add", (rcol[:, 2:3], w0, w1), reads=(t_wst,), writes=(t_rcol,))
        b.op("vector", "tensor_add", (rcol[:, 3:4], w2, w3), reads=(t_wst,), writes=(t_rcol,))
        b.op("vector", "tensor_add", (rcol[:, 2:3], rcol[:, 2:3], rcol[:, 3:4]),
             reads=(t_rcol,), writes=(t_rcol,))
        b.op("vector", "tensor_scalar_add", (rcol[:, 2:3], rcol[:, 2:3], 1e-20),
             reads=(t_rcol,), writes=(t_rcol,))
        b.op("vector", "reciprocal", (rcol[:, 3:4], rcol[:, 2:3]),
             reads=(t_rcol,), writes=(t_rcol,))
        for k in range(4):
            wk = wstore[:, i * 4 + k:i * 4 + k + 1]
            b.op("vector", "scalar_tensor_tensor",
                 (wk, wk, SCALE, rcol[:, 3:4], ALU.mult, ALU.mult),
                 reads=(t_wst, t_rcol), writes=(t_wst,))
        b.op("vector", "tensor_copy",
             (selbf[:, i * E:(i + 1) * E], self32[:, i * E:(i + 1) * E]),
             reads=(t_self32,), writes=(t_selbf,))
        j = i // 2
        s2 = i % 2
        b.op("tensor", "ldweights", (c_ocol[:],), reads=(t_ocol,))
        b.op("tensor", "matmul",
             (prow[0:1, j * E:(j + 1) * E], c_ocol[:], selbf[:, i * E:(i + 1) * E]),
             kwargs=dict(start=(s2 == 0), stop=(s2 == 1)),
             reads=(t_ocol, t_selbf), writes=(t_prow,))

    for j in range(NBLK):
        for s2 in range(2):
            i = 2 * j + s2
            dma("sync", x_t[:], x_own[i * 128:(i + 1) * 128, :], writes=(t_x,))
            dma("sync", tabc[:], cos_own[i * 128:(i + 1) * 128, :], writes=(t_tabc,))
            dma("sync", tabs_[:], sin_own[i * 128:(i + 1) * 128, :], writes=(t_tabs,))
            ln_tile()
            if SUB == "ln" and i == 0:
                return finish()
            if SUB == "lnall2":
                b.op("vector", "tensor_copy", (qbf[:], xn_t[:]),
                     reads=(t_xn,), writes=(t_qbf,))
                dma("sync", xn_tm[i * 128:(i + 1) * 128, :], qbf[:],
                    reads=(t_qbf,), writes=(d_xn,))
                continue
            dma("gpsimd", xn_tm[i * 128:(i + 1) * 128, :], xn_t[:],
                reads=(t_xn,), writes=(d_xn,))
            if SUB == "xnst" and i == 0:
                return finish()
            if SUB in ("lnall",):
                continue
            if SUB == "trnof32" and i == 0:
                transpose_xn(False)
                return finish()
            transpose_xn(True)
            if SUB == "tr" and i == 0:
                return finish()
            dma("sync", xnfm_d[i], xnfbf[:].rearrange("p (c q) -> p c q", c=16),
                reads=(t_xnfbf,), writes=(d_xnfm,))
            for c in range(16):
                b.op("tensor", "matmul",
                     (py[:, 0:E], xnf32[:, c * 128:(c + 1) * 128],
                      gate_sb[:, c * E:(c + 1) * E]),
                     kwargs=dict(start=(c == 0), stop=(c == 15)),
                     reads=(t_xnf32, t_gate), writes=(t_py,))
            if SUB == "gate" and i == 0:
                return finish()
            routing_r1(i)
            if SUB == "rout" and i == 0:
                return finish()
            kv_pass()
            if SUB == "kv" and i == 0:
                return finish()
            dma("sync", vst[i * 128:(i + 1) * 128, :], vbf_s[:],
                reads=(t_vbf,), writes=(d_v,))
            rope_rms(k32[:], 512, HKV, False, kbf[:], t_kbf, t_k32)
            dma("sync", kst[i * 128:(i + 1) * 128, :], kbf[:],
                reads=(t_kbf,), writes=(d_k,))
            if SUB == "krope" and i == 0:
                return finish()
            for c in range(16):
                b.op("tensor", "ldweights", (xnfbf[:, c * 128:(c + 1) * 128],),
                     reads=(t_xnfbf,))
                for f in range(4):
                    b.op("tensor", "matmul",
                         (pm[:, f * 512:(f + 1) * 512],
                          xnfbf[:, c * 128:(c + 1) * 128],
                          wbuf[:, c * 3072 + f * 512: c * 3072 + (f + 1) * 512]),
                         kwargs=dict(start=(c == 0), stop=(c == 15)),
                         reads=(t_xnfbf,) + wbq(0, 49152), writes=(t_pm,))
            rope_rms(pm[:], D, HQ, True, qbf[:], t_qbf, t_pm)
            dma("sync", qst[i * 128:(i + 1) * 128, :], qbf[:],
                reads=(t_qbf,), writes=(d_q,))
            if SUB == "qrope" and i == 0:
                return finish()
        for hh in range(4):
            if SUB in ("lnall",):
                continue
            r0 = j * WIN + hh * 128
            dma("sync", x_t[:], x_halo[r0:r0 + 128, :], writes=(t_x,))
            dma("sync", tabc[:], cos_h[r0:r0 + 128, :], writes=(t_tabc,))
            dma("sync", tabs_[:], sin_h[r0:r0 + 128, :], writes=(t_tabs,))
            ln_tile()
            transpose_xn(False)
            kv_pass()
            dma("sync", vhst[r0:r0 + 128, :], vbf_s[:], reads=(t_vbf,), writes=(d_vh,))
            rope_rms(k32[:], 512, HKV, False, kbf[:], t_kbf, t_k32)
            dma("sync", khst[r0:r0 + 128, :], kbf[:], reads=(t_kbf,), writes=(d_kh,))

    if nphase < 2:
        return finish()
    b.op("vector", "tensor_copy", (cnt_row[0:1, 0:NBLK * E], prow[0:1, 0:NBLK * E]),
         reads=(t_prow,), writes=(t_cnt_row,))
    dma("sync", offs_d, cnt_row[0:1, 0:NBLK * E], reads=(t_cnt_row,), writes=(d_of,))
    dma("sync", cnt64[0:NBLK, :], offs_d, reads=(d_of,), writes=(t_c64,))
    b.op("vector", "tensor_copy", (cnt64b[0:NBLK, :], cnt64[0:NBLK, :]),
         reads=(t_c64,), writes=(t_c64b,))
    b.op("tensor", "ldweights", (place_sb[:],), reads=(t_place,))
    b.op("tensor", "matmul", (pl64, place_sb[:], cnt64b[0:NBLK, :]),
         kwargs=dict(start=True, stop=True),
         reads=(t_place, t_c64b), writes=(t_pl64,))
    b.op("vector", "tensor_copy", (cnt64f[:], pl64), reads=(t_pl64,), writes=(t_c64f,))
    dma("sync", cc_in, cnt64f[:], reads=(t_c64f,), writes=(d_ci,))
    b.op("gpsimd", "collective_compute",
         ("AllReduce", ALU.add, [list(range(8))], [cc_in], [cc_out]),
         reads=(d_ci,), writes=(d_co,), inc_sem="cc", inc=1)

    if nphase < 3:
        return finish()
    b.barrier()
    wp = wbuf[:, 0:32768].rearrange("p (c f) -> p c f", c=16)
    dma("gpsimd", wp, wprojT.rearrange("(c p) f -> p c f", p=128),
        writes=wbq(0, 32768))

    for j in range(NBLK):
        for kt in range(4):
            dma("sync", kin[:, kt * 512:(kt + 1) * 512],
                khst[j * WIN + kt * 128: j * WIN + (kt + 1) * 128, :],
                reads=(d_kh,), writes=(t_kin,))
            dma("sync", vin[:, kt * 512:(kt + 1) * 512],
                vhst[j * WIN + kt * 128: j * WIN + (kt + 1) * 128, :],
                reads=(d_vh,), writes=(t_vin,))
        for kt in (4, 5):
            r0 = j * 256 + (kt - 4) * 128
            dma("sync", kin[:, kt * 512:(kt + 1) * 512], kst[r0:r0 + 128, :],
                reads=(d_k,), writes=(t_kin,))
            dma("sync", vin[:, kt * 512:(kt + 1) * 512], vst[r0:r0 + 128, :],
                reads=(d_v,), writes=(t_vin,))
        dma("sync", masksb[:].rearrange("p (t q) -> p t q", t=6),
            masksT[j * 768:(j + 1) * 768, :].rearrange("(t p) q -> p t q", p=128),
            writes=(t_msk,))
        uu = [0]

        def tp_unit(src_ap, dst_ap, treads, twrites):
            u = uu[0] % 4
            b.op("tensor", "matmul", (pbf_ps[:, u * 128:(u + 1) * 128], src_ap,
                                      c_eyebf[:]),
                 kwargs=dict(start=True, stop=True),
                 reads=treads + (t_eyebf,), writes=(t_pbf,))
            b.op("vector", "tensor_copy", (dst_ap, pbf_ps[:, u * 128:(u + 1) * 128]),
                 reads=(t_pbf,), writes=twrites)
            uu[0] += 1

        for s2 in range(2):
            dma("sync", qin[:], qst[j * 256 + s2 * 128: j * 256 + (s2 + 1) * 128, :],
                reads=(d_q,), writes=(t_qin,))
            for h in range(HQ):
                tp_unit(qin[:, h * 128:(h + 1) * 128],
                        qfm[:, h * 256 + s2 * 128: h * 256 + (s2 + 1) * 128],
                        (t_qin,), (t_qfm,))
        for kt in range(6):
            for kv in range(HKV):
                tp_unit(kin[:, kt * 512 + kv * 128:kt * 512 + (kv + 1) * 128],
                        kfm[:, kv * 768 + kt * 128: kv * 768 + (kt + 1) * 128],
                        (t_kin,), (t_kfm,))
        for h in range(HQ):
            kv = h // 4
            for kt in range(6):
                sl = kt % 2
                b.op("tensor", "ldweights",
                     (kfm[:, kv * 768 + kt * 128: kv * 768 + (kt + 1) * 128],),
                     reads=(t_kfm,))
                b.op("tensor", "matmul",
                     (psc[:, sl * 256:(sl + 1) * 256],
                      kfm[:, kv * 768 + kt * 128: kv * 768 + (kt + 1) * 128],
                      qfm[:, h * 256:(h + 1) * 256]),
                     kwargs=dict(start=True, stop=True),
                     reads=(t_kfm, t_qfm), writes=(t_psc,))
                b.op("scalar", "activation",
                     (probs[:, kt * 256:(kt + 1) * 256], psc[:, sl * 256:(sl + 1) * 256],
                      AF.Exp),
                     kwargs=dict(bias=c_mb[:]),
                     reads=(t_psc, t_mb), writes=(t_pr,))
                b.op("vector", "tensor_mul",
                     (probs[:, kt * 256:(kt + 1) * 256],
                      probs[:, kt * 256:(kt + 1) * 256],
                      masksb[:, kt * 256:(kt + 1) * 256]),
                     reads=(t_pr, t_msk), writes=(t_pr,))
                b.op("tensor", "ldweights", (c_ocol[:],), reads=(t_ocol,))
                b.op("tensor", "matmul",
                     (prow[0:1, 256:512], c_ocol[:], probs[:, kt * 256:(kt + 1) * 256]),
                     kwargs=dict(start=(kt == 0), stop=(kt == 5)),
                     reads=(t_ocol, t_pr), writes=(t_prow,))
                b.op("tensor", "ldweights",
                     (vin[:, kt * 512 + kv * 128:kt * 512 + (kv + 1) * 128],),
                     reads=(t_vin,))
                b.op("tensor", "matmul",
                     (py[:, 0:256],
                      vin[:, kt * 512 + kv * 128:kt * 512 + (kv + 1) * 128],
                      probs[:, kt * 256:(kt + 1) * 256]),
                     kwargs=dict(start=(kt == 0), stop=(kt == 5)),
                     reads=(t_vin, t_pr), writes=(t_py,))
            b.op("vector", "tensor_copy", (yraw[:, h * 256:(h + 1) * 256], py[:, 0:256]),
                 reads=(t_py,), writes=(t_yraw,))
            b.op("vector", "tensor_copy", (denrow[:], prow[0:1, 256:512]),
                 reads=(t_prow,), writes=(t_den,))
            b.op("scalar", "activation", (recrow[:], denrow[:], AF.Ln),
                 reads=(t_den,), writes=(t_rec,))
            b.op("scalar", "activation", (recrow[:], recrow[:], AF.Exp),
                 kwargs=dict(scale=c_neg1[0:1, :]), reads=(t_rec, t_neg1), writes=(t_rec,))
            b.op("tensor", "matmul",
                 (psc[:, 0:256], c_orow32[:], recrow[:]),
                 kwargs=dict(start=True, stop=True),
                 reads=(t_orow32, t_rec), writes=(t_psc,))
            b.op("vector", "tensor_copy", (recbc[:], psc[:, 0:256]),
                 reads=(t_psc,), writes=(t_rbc,))
            b.op("vector", "tensor_mul",
                 (yraw[:, h * 256:(h + 1) * 256], yraw[:, h * 256:(h + 1) * 256],
                  recbc[:]),
                 reads=(t_yraw, t_rbc), writes=(t_yraw,))
        for tt in range(2):
            for h in range(HQ):
                b.op("tensor", "ldweights",
                     (yraw[:, h * 256 + tt * 128: h * 256 + (tt + 1) * 128],),
                     reads=(t_yraw,))
                for f in range(4):
                    b.op("tensor", "matmul",
                         (pm[:, f * 512:(f + 1) * 512],
                          yraw[:, h * 256 + tt * 128: h * 256 + (tt + 1) * 128],
                          wbuf[:, h * 2048 + f * 512: h * 2048 + (f + 1) * 512]),
                         kwargs=dict(start=(h == 0), stop=(h == 15)),
                         reads=(t_yraw,) + wbq(0, 32768), writes=(t_pm,))
            b.op("vector", "tensor_copy", (sbf[:], pm[:]), reads=(t_pm,), writes=(t_sbf,))
            dma("sync", s_st[j * 256 + tt * 128: j * 256 + (tt + 1) * 128, :], sbf[:],
                reads=(t_sbf,), writes=(d_s,))

    if nphase < 4:
        return finish()
    b.barrier()
    for qtr in range(4):
        wu = wbuf[:, 0:16384].rearrange("p (c f) -> p c f", c=16)
        dma("gpsimd", wu,
            shupT.rearrange("(c p) f -> p c f", p=128)[:, :, qtr * 1024:(qtr + 1) * 1024],
            writes=wbq(0, 16384))
        wd = wbuf[:, 16384:32768].rearrange("p (c f) -> p c f", c=8)
        dma("gpsimd", wd,
            shdnT[qtr * 1024:(qtr + 1) * 1024, :].rearrange("(c p) f -> p c f", p=128),
            writes=wbq(16384, 32768))
        for g in range(4):
            dma("sync", xng[:].rearrange("p (c t q) -> p c t q", c=16, t=4),
                xnfm_d[4 * g:4 * g + 4].transpose([1, 2, 0, 3]),
                reads=(d_xnfm,), writes=(t_xng,))
            for hsc in range(8):
                for c in range(16):
                    b.op("tensor", "ldweights",
                         (wbuf[:, c * 1024 + hsc * 128: c * 1024 + (hsc + 1) * 128],),
                         reads=wbq(0, 16384))
                    b.op("tensor", "matmul",
                         (py[:, 0:512],
                          wbuf[:, c * 1024 + hsc * 128: c * 1024 + (hsc + 1) * 128],
                          xng[:, c * 512:(c + 1) * 512]),
                         kwargs=dict(start=(c == 0), stop=(c == 15)),
                         reads=wbq(0, 16384) + (t_xng,), writes=(t_py,))
                b.op("scalar", "activation", (relu32[:], py[:, 0:512], AF.Relu),
                     reads=(t_py,), writes=(t_rel,))
                b.op("vector", "tensor_mul",
                     (hfm[:, hsc * 512:(hsc + 1) * 512], relu32[:], relu32[:]),
                     reads=(t_rel,), writes=(t_hfm,))
            for tt in range(4):
                for hsc in range(8):
                    b.op("tensor", "ldweights",
                         (hfm[:, hsc * 512 + tt * 128: hsc * 512 + (tt + 1) * 128],),
                         reads=(t_hfm,))
                    for f in range(4):
                        b.op("tensor", "matmul",
                             (pm[:, f * 512:(f + 1) * 512],
                              hfm[:, hsc * 512 + tt * 128: hsc * 512 + (tt + 1) * 128],
                              wbuf[:, 16384 + hsc * 2048 + f * 512:
                                   16384 + hsc * 2048 + (f + 1) * 512]),
                             kwargs=dict(start=(hsc == 0), stop=(hsc == 7)),
                             reads=(t_hfm,) + wbq(16384, 32768), writes=(t_pm,))
                b.op("vector", "tensor_copy", (f32a[:], pm[:]),
                     reads=(t_pm,), writes=(t_fa,))
                r0 = g * 512 + tt * 128
                if qtr == 0:
                    dma("sync", shacc[r0:r0 + 128, :], f32a[:],
                        reads=(t_fa,), writes=(d_sh,))
                else:
                    dma("gpsimd", shacc[r0:r0 + 128, :], f32a[:],
                        reads=(t_fa,), writes=(d_sh,), accum_op=ALU.add)

    if nphase < 5:
        return finish()
    b.barrier()
    dma("sync", cnt64[:], cc_out, reads=(d_co,), writes=(t_c64,),
        deps=[("cc", b.cnt["cc"])])
    b.op("vector", "tensor_copy", (cnt64b[:], cnt64[:]), reads=(t_c64,), writes=(t_c64b,))
    b.op("tensor", "ldweights", (mm_sb[:],), reads=(t_mm,))
    b.op("tensor", "matmul", (poff, mm_sb[:], cnt64b[:]),
         kwargs=dict(start=True, stop=True), reads=(t_mm, t_c64b), writes=(t_poff,))
    b.op("vector", "tensor_copy", (offs_sb[:], poff), reads=(t_poff,), writes=(t_offs,))
    dma("sync", offs_d, offs_sb[:], reads=(t_offs,), writes=(d_of,))
    for j in range(NBLK):
        dma("sync", obc[:], offs_d[j:j + 1, :].broadcast_to([128, E]),
            reads=(d_of,), writes=(t_obc,))
        for s2 in range(2):
            i = 2 * j + s2
            b.op("tensor", "ldweights", (c_lt[:],), reads=(t_lt,))
            b.op("tensor", "matmul", (py[:, 0:E], c_lt[:], selbf[:, i * E:(i + 1) * E]),
                 kwargs=dict(start=True, stop=(s2 == 0)),
                 reads=(t_lt, t_selbf), writes=(t_py,))
            if s2 == 1:
                b.op("tensor", "ldweights", (c_o128[:],), reads=(t_o128,))
                b.op("tensor", "matmul",
                     (py[:, 0:E], c_o128[:], selbf[:, (i - 1) * E:i * E]),
                     kwargs=dict(start=False, stop=True),
                     reads=(t_o128, t_selbf), writes=(t_py,))
            b.op("vector", "tensor_add", (r32a[:], py[:, 0:E], obc[:]),
                 reads=(t_py, t_obc), writes=(t_r32a,))
            b.op("vector", "tensor_scalar",
                 (r32b[:], r32a[:], float(CAP), None, ALU.is_lt),
                 reads=(t_r32a,), writes=(t_r32b,))
            b.op("vector", "tensor_mul",
                 (vld32[:, i * E:(i + 1) * E], r32b[:], self32[:, i * E:(i + 1) * E]),
                 reads=(t_r32b, t_self32), writes=(t_vld32,))
            b.op("vector", "tensor_copy",
                 (vldbf[:, i * E:(i + 1) * E], vld32[:, i * E:(i + 1) * E]),
                 reads=(t_vld32,), writes=(t_vldbf,))
    for i in range(NT):
        b.op("tensor", "ldweights", (c_lt[:],), reads=(t_lt,))
        b.op("tensor", "matmul", (py[:, E:2 * E], c_lt[:], vldbf[:, i * E:(i + 1) * E]),
             kwargs=dict(start=True, stop=(i == 0)),
             reads=(t_lt, t_vldbf), writes=(t_py,))
        if i > 0:
            b.op("tensor", "ldweights", (c_o128[:],), reads=(t_o128,))
            for jj in range(i):
                b.op("tensor", "matmul",
                     (py[:, E:2 * E], c_o128[:], vldbf[:, jj * E:(jj + 1) * E]),
                     kwargs=dict(start=False, stop=(jj == i - 1)),
                     reads=(t_o128, t_vldbf), writes=(t_py,))
        b.op("vector", "tensor_copy", (r32c[:], py[:, E:2 * E]),
             reads=(t_py,), writes=(t_r32c,))
        b.op("vector", "tensor_scalar_add", (tloc[:], c_iota128[:], float(i * 128)),
             reads=(t_iota128,), writes=(t_tloc,))
        for k in range(4):
            ek = estore[:, i * 4 + k:i * 4 + k + 1]
            eq = r32a
            b.op("vector", "tensor_scalar",
                 (eq[:], c_iota32[:], ek, None, ALU.is_equal),
                 reads=(t_iota32, t_est), writes=(t_r32a,))
            b.op("vector", "tensor_mul", (r32b[:], eq[:], r32c[:]),
                 reads=(t_r32a, t_r32c), writes=(t_r32b,))
            b.op("vector", "reduce_sum", (rcol[:, 4:5], r32b[:]),
                 kwargs=dict(axis=AX.X), reads=(t_r32b,), writes=(t_rcol,))
            b.op("vector", "tensor_mul", (r32b[:], eq[:], vld32[:, i * E:(i + 1) * E]),
                 reads=(t_r32a, t_vld32), writes=(t_r32b,))
            b.op("vector", "reduce_sum", (rcol[:, 5:6], r32b[:]),
                 kwargs=dict(axis=AX.X), reads=(t_r32b,), writes=(t_rcol,))
            b.op("vector", "scalar_tensor_tensor",
                 (rcol[:, 6:7], ek, float(L), rcol[:, 4:5], ALU.mult, ALU.add),
                 reads=(t_est, t_rcol), writes=(t_rcol,))
            b.op("vector", "tensor_scalar_add", (rcol[:, 7:8], tloc[:], BIGA),
                 reads=(t_tloc,), writes=(t_rcol,))
            b.op("vector", "tensor_scalar",
                 (rcol[:, 8:9], rcol[:, 5:6], -1.0, 1.0, ALU.mult, ALU.add),
                 reads=(t_rcol,), writes=(t_rcol,))
            b.op("vector", "tensor_mul", (rcol[:, 6:7], rcol[:, 6:7], rcol[:, 5:6]),
                 reads=(t_rcol,), writes=(t_rcol,))
            b.op("vector", "tensor_mul", (rcol[:, 7:8], rcol[:, 7:8], rcol[:, 8:9]),
                 reads=(t_rcol,), writes=(t_rcol,))
            b.op("vector", "tensor_add", (rcol[:, 6:7], rcol[:, 6:7], rcol[:, 7:8]),
                 reads=(t_rcol,), writes=(t_rcol,))
            b.op("vector", "tensor_copy", (addrs[:, i * 4 + k:i * 4 + k + 1],
                                           rcol[:, 6:7]),
                 reads=(t_rcol,), writes=(t_addr,))
            sd_ap, t_sd = sdats[k]
            b.op("vector", "tensor_copy", (sd_ap[:, 0:1], tloc[:]),
                 reads=(t_tloc,), writes=(t_sd,))
            b.op("vector", "tensor_copy", (sd_ap[:, 1:2],
                                           wstore[:, i * 4 + k:i * 4 + k + 1]),
                 reads=(t_wst,), writes=(t_sd,))
            b.op("gpsimd", "indirect_dma_start",
                 kwargs=dict(out=lists_d,
                             out_offset=IndirectOffsetOnAxis(
                                 addrs[:, i * 4 + k:i * 4 + k + 1], 0),
                             in_=sd_ap[:], in_offset=None,
                             bounds_check=E * L - 1, oob_is_err=False),
                 reads=(t_sd, t_addr), writes=(d_li,), dma=True)

    if nphase < 6:
        return finish()
    b.barrier()
    b.op("vector", "memset", (xe_tm[:], 0.0), writes=(t_xet,))
    upslot = 0
    for e in range(E):
        dnslot = (upslot + 1) % 3
        upo = upslot * 16384
        dno = dnslot * 16384
        wu = wbuf[:, upo:upo + 16384].rearrange("p (c f) -> p c f", c=16)
        dma("gpsimd", wu, upT[e].rearrange("(c p) f -> p c f", p=128),
            writes=wbq(upo, upo + 16384))
        wd = wbuf[:, dno:dno + 16384].rearrange("p (c f) -> p c f", c=8)
        dma("gpsimd", wd, dnT[e].rearrange("(c p) f -> p c f", p=128),
            writes=wbq(dno, dno + 16384))
        for stile in range(2):
            r0 = e * L + stile * 128
            dma("sync", lists_t[:, stile * 2:stile * 2 + 2],
                lists_d[r0:r0 + 128, :], reads=(d_li,), writes=(t_lit,))
            b.op("vector", "tensor_copy",
                 (idx_u[:, stile:stile + 1], lists_t[:, stile * 2:stile * 2 + 1]),
                 reads=(t_lit,), writes=(t_idx,))
            b.op("gpsimd", "indirect_dma_start",
                 kwargs=dict(out=xe_tm[:], out_offset=None,
                             in_=xn_tm,
                             in_offset=IndirectOffsetOnAxis(idx_u[:, stile:stile + 1], 0),
                             bounds_check=TOK - 1, oob_is_err=False),
                 reads=(d_xn, t_idx), writes=(t_xet,), dma=True)
            for c4 in range(4):
                for u in range(4):
                    c = c4 * 4 + u
                    b.op("tensor", "matmul",
                         (pbf_ps[:, u * 128:(u + 1) * 128],
                          xe_tm[:, c * 128:(c + 1) * 128], c_eyebf[:]),
                         kwargs=dict(start=True, stop=True),
                         reads=(t_xet, t_eyebf), writes=(t_pbf,))
                for u in range(4):
                    c = c4 * 4 + u
                    b.op("vector", "tensor_copy",
                         (xe_fm[:, c * 256 + stile * 128: c * 256 + (stile + 1) * 128],
                          pbf_ps[:, u * 128:(u + 1) * 128]),
                         reads=(t_pbf,), writes=(t_xef,))
        for hf in range(8):
            sl = hf % 2
            for c in range(16):
                b.op("tensor", "ldweights",
                     (wbuf[:, upo + c * 1024 + hf * 128: upo + c * 1024 + (hf + 1) * 128],),
                     reads=wbq(upo, upo + 16384))
                b.op("tensor", "matmul",
                     (py[:, sl * 256:(sl + 1) * 256],
                      wbuf[:, upo + c * 1024 + hf * 128: upo + c * 1024 + (hf + 1) * 128],
                      xe_fm[:, c * 256:(c + 1) * 256]),
                     kwargs=dict(start=(c == 0), stop=(c == 15)),
                     reads=wbq(upo, upo + 16384) + (t_xef,), writes=(t_py,))
            b.op("scalar", "activation", (relu32[:, 0:256], py[:, sl * 256:(sl + 1) * 256],
                                          AF.Relu),
                 reads=(t_py,), writes=(t_rel,))
            b.op("vector", "tensor_mul",
                 (he_bf[:, hf * 256:(hf + 1) * 256], relu32[:, 0:256], relu32[:, 0:256]),
                 reads=(t_rel,), writes=(t_he,))
        for stile in range(2):
            for hf in range(8):
                b.op("tensor", "ldweights",
                     (he_bf[:, hf * 256 + stile * 128: hf * 256 + (stile + 1) * 128],),
                     reads=(t_he,))
                for f in range(4):
                    b.op("tensor", "matmul",
                         (pm[:, f * 512:(f + 1) * 512],
                          he_bf[:, hf * 256 + stile * 128: hf * 256 + (stile + 1) * 128],
                          wbuf[:, dno + hf * 2048 + f * 512: dno + hf * 2048 + (f + 1) * 512]),
                         kwargs=dict(start=(hf == 0), stop=(hf == 7)),
                         reads=(t_he,) + wbq(dno, dno + 16384), writes=(t_pm,))
            b.op("vector", "tensor_scalar_mul",
                 (ye_sb[:], pm[:], lists_t[:, stile * 2 + 1:stile * 2 + 2]),
                 reads=(t_pm, t_lit), writes=(t_ye,))
            dma("sync", ye_all[e * L + stile * 128: e * L + (stile + 1) * 128, :],
                ye_sb[:], reads=(t_ye,), writes=(d_ye,))
        upslot = (upslot + 2) % 3

    if nphase < 7:
        return finish()
    b.barrier()
    g4 = wbuf
    t_g4 = t_ws[0]
    for i in range(NT):
        r0 = i * 128
        for k in range(4):
            b.op("vector", "memset", (g4[:, k * D:(k + 1) * D], 0.0), writes=(t_g4,))
            b.op("gpsimd", "indirect_dma_start",
                 kwargs=dict(out=g4[:, k * D:(k + 1) * D], out_offset=None,
                             in_=ye_all,
                             in_offset=IndirectOffsetOnAxis(
                                 addrs[:, i * 4 + k:i * 4 + k + 1], 0),
                             bounds_check=E * L - 1, oob_is_err=False),
                 reads=(d_ye, t_addr), writes=(t_g4,), dma=True)
        b.op("vector", "tensor_add", (f32a[:], g4[:, 0:D], g4[:, D:2 * D]),
             reads=(t_g4,), writes=(t_fa,))
        b.op("vector", "tensor_add", (f32b[:], g4[:, 2 * D:3 * D], g4[:, 3 * D:4 * D]),
             reads=(t_g4,), writes=(t_fb,))
        b.op("vector", "tensor_add", (f32a[:], f32a[:], f32b[:]),
             reads=(t_fa, t_fb), writes=(t_fa,))
        dma("sync", shal[:], shacc[r0:r0 + 128, :], reads=(d_sh,), writes=(t_shal,))
        b.op("vector", "tensor_add", (f32a[:], f32a[:], shal[:]),
             reads=(t_fa, t_shal), writes=(t_fa,))
        b.op("scalar", "activation", (f32b[:], f32a[:], AF.Tanh),
             kwargs=dict(scale=c_half[:]), reads=(t_fa, t_half), writes=(t_fb,))
        b.op("vector", "tensor_scalar",
             (f32b[:], f32b[:], 0.5, 0.5, ALU.mult, ALU.add),
             reads=(t_fb,), writes=(t_fb,))
        b.op("vector", "tensor_mul", (f32b[:], f32b[:], f32a[:]),
             reads=(t_fa, t_fb), writes=(t_fb,))
        dma("gpsimd", sld[:], s_st[r0:r0 + 128, :], reads=(d_s,), writes=(t_sld,))
        b.op("vector", "tensor_mul", (f32b[:], f32b[:], sld[:]),
             reads=(t_fb, t_sld), writes=(t_fb,))
        dma("sync", x_t[:], x_own[r0:r0 + 128, :], writes=(t_x,))
        b.op("vector", "tensor_add", (f32b[:], f32b[:], x_t[:]),
             reads=(t_fb, t_x), writes=(t_fb,))
        dma("sync", out_d[r0:r0 + 128, :], f32b[:], reads=(t_fb,), writes=(d_out,))

    return finish()


def _emit_debug(nc, b, dma, env):
    pairs = [
        ("dbg_xn", "xn_tm", "d_xn", [TOK, D], BF16),
        ("dbg_q", "qst", "d_q", [TOK, D], BF16),
        ("dbg_k", "kst", "d_k", [TOK, 512], BF16),
        ("dbg_v", "vst", "d_v", [TOK, 512], BF16),
        ("dbg_kh", "khst", "d_kh", [NBLK * WIN, 512], BF16),
        ("dbg_vh", "vhst", "d_vh", [NBLK * WIN, 512], BF16),
        ("dbg_s", "s_st", "d_s", [TOK, D], BF16),
        ("dbg_sh", "shacc", "d_sh", [TOK, D], F32),
        ("dbg_li", "lists_d", "d_li", [E * L, 2], F32),
        ("dbg_ye", "ye_all", "d_ye", [E * L, D], BF16),
        ("dbg_cc", "cc_out", "d_co", [64, E], F32),
        ("dbg_of", "offs_d", "d_of", [NBLK, E], F32),
    ]
    for oname, sname, tname, shape, dt in pairs:
        o = nc.dram_tensor(oname, shape, dt, kind="ExternalOutput").ap()
        dma("sync", o, env[sname], reads=(env[tname],))
    for oname, sname, tname, shape, dt in [
            ("dbg_addr", "addrs", "t_addr", [128, NT * 4], U32),
            ("dbg_est", "estore", "t_est", [128, NT * 4], F32),
            ("dbg_wst", "wstore", "t_wst", [128, NT * 4], F32),
            ("dbg_sel", "self32", "t_self32", [128, NT * E], F32),
            ("dbg_vld", "vld32", "t_vld32", [128, NT * E], F32)]:
        o = nc.dram_tensor(oname, shape, dt, kind="ExternalOutput").ap()
        dma("sync", o, env[sname][:], reads=(env[tname],))


def _build_finish(nc, b, st):
    sems = {}
    sem_cms = {}
    for name in list(b.cnt.keys()):
        cm = nc.semaphore("sem_" + name)
        sems[name] = cm.__enter__()
        sem_cms[name] = cm

    block_cm = nc.Block()
    block = block_cm.__enter__()

    plans = {e: b.plan_for(e) for e in B.ENGINES}
    final_cnt = dict(b.cnt)

    def emit_engine(engine_obj, ename):
        bregs = {}
        if ename == "gpsimd":
            for bv in (E * L - 1, TOK - 1):
                r = engine_obj.alloc_register()
                engine_obj.reg_mov(r, bv)
                bregs[bv] = r
        for (waits, dmaflag, method, args, kwargs, amt, sem_key) in plans[ename]:
            if method == "indirect_dma_start" and isinstance(
                    kwargs.get("bounds_check"), int):
                kwargs = {**kwargs, "bounds_check": bregs[kwargs["bounds_check"]]}
            for (s, v) in waits:
                engine_obj.wait_ge(sems[s], v)
            if method is None:
                continue
            fn = getattr(engine_obj, method)
            try:
                inst = fn(*args, **kwargs)
            except Exception:
                import sys
                print("EMIT FAIL:", ename, method,
                      {k: (str(v)[:120]) for k, v in kwargs.items()},
                      [str(a)[:120] for a in args], file=sys.stderr)
                raise
            if sem_key == "cc":
                inst.then_inc(sems["cc"])
            elif dmaflag:
                inst.then_inc(sems[sem_key], amt)
            else:
                inst.then_inc(sems[sem_key], amt, skip_validation=True)
        if ename == "sync":
            for sname, v in final_cnt.items():
                if v > 0 and sname not in ("tensor", "vector", "scalar"):
                    engine_obj.wait_ge(sems[sname], v)

    @block.tensor
    def _(e):
        emit_engine(e, "tensor")

    @block.vector
    def _(e):
        emit_engine(e, "vector")

    @block.scalar
    def _(e):
        emit_engine(e, "scalar")

    @block.sync
    def _(e):
        emit_engine(e, "sync")

    @block.gpsimd
    def _(e):
        emit_engine(e, "gpsimd")

    block_cm.__exit__(None, None, None)
    for cm in sem_cms.values():
        cm.__exit__(None, None, None)
    st.close()
    return nc


_NC_CACHE = {}


def kernel(**inputs):
    in_maps, core_blocks = _host_prep(inputs)
    if "nc" not in _NC_CACHE:
        _NC_CACHE["nc"] = build_nc()
    nc = _NC_CACHE["nc"]
    res = run_bass_kernel_spmd(nc, in_maps, core_ids=list(range(NCORES)))
    out = np.zeros((16384, D), np.float32)
    for c in range(NCORES):
        oc = res.results[c]["out"]
        for j, g in enumerate(core_blocks[c]):
            out[g * BLK:(g + 1) * BLK] = oc[j * BLK:(j + 1) * BLK]
    return out.reshape(4, 4096, D)
